# version 02
# baseline (speedup 1.0000x reference)
import numpy as np
import ml_dtypes
from contextlib import ExitStack

import concourse.bass as bass
import concourse.mybir as mybir
from concourse.bass_utils import run_bass_kernel_spmd

F32 = mybir.dt.float32
BF16 = mybir.dt.bfloat16
AF = mybir.ActivationFunctionType
ALU = mybir.AluOpType
AX = mybir.AxisListType
BF = ml_dtypes.bfloat16

NCORES = 8
B, T, D = 2, 8192, 1024
TPC = 2048
NCH = D // 128
NTT = TPC // 512

_NC_CACHE = {}


class Prog:

    ENGS = ("T", "V", "S", "G", "Y")

    def __init__(self, nc, es):
        self.nc = nc
        self.es = es
        self.q = {e: [] for e in self.ENGS}
        self.sem = {e: es.enter_context(nc.semaphore(f"sem{e}")) for e in self.ENGS}
        self.cnt = {e: 0 for e in self.ENGS}
        self.gsem = {}

    def op(self, eng, fn, waits=(), inc=True):
        waits = [w for w in waits if w is not None]
        tok = None
        if inc:
            self.cnt[eng] += 1
            tok = (eng, self.cnt[eng])
        self.q[eng].append((fn, tuple(waits), tok))
        return tok

    def dma(self, eng, out, in_, waits=(), group=None, **kw):
        assert group is not None
        waits = [w for w in waits if w is not None]
        if group not in self.gsem:
            self.gsem[group] = [self.es.enter_context(self.nc.semaphore(f"g_{group}")), 0]
        ent = self.gsem[group]
        ent[1] += 16
        tok = ("g:" + group, ent[1])
        sem = ent[0]

        def fn(e, out=out, in_=in_, kw=kw):
            return e.dma_start(out=out, in_=in_, **kw).then_inc(sem, 16)

        self.q[eng].append((fn, tuple(waits), None))
        return tok

    def gtotal(self, group):
        return ("g:" + group, self.gsem[group][1])

    def collective(self, kind, ins, outs, waits=()):
        waits = [w for w in waits if w is not None]
        self.cnt["G"] += 1
        tok = ("G", self.cnt["G"])
        sem = self.sem["G"]

        def fn(e, ins=ins, outs=outs):
            return e.collective_compute(
                kind, ALU.bypass, replica_groups=[list(range(NCORES))],
                ins=ins, outs=outs)

        def fn2(e):
            return fn(e).then_inc(sem, 1)
        self.q["G"].append(((lambda e: fn(e).then_inc(sem, 1)), tuple(waits), None))
        return tok

    def _resolve(self, t):
        name, v = t
        if name.startswith("g:"):
            return self.gsem[name[2:]][0], v
        return self.sem[name], v

    def emit(self):
        for gname, (sem, cnt) in self.gsem.items():
            self.q["Y"].append(((lambda e: None), (("g:" + gname, cnt),), None))
        block = self.es.enter_context(self.nc.Block())

        def body(engine, key):
            for fn, waits, tok in self.q[key]:
                for t in waits:
                    s, v = self._resolve(t)
                    engine.wait_ge(s, v)
                ins = fn(engine)
                if tok is not None:
                    ins.then_inc(self.sem[tok[0]], 1)

        block.tensor(lambda e: body(e, "T"))
        block.vector(lambda e: body(e, "V"))
        block.scalar(lambda e: body(e, "S"))
        block.gpsimd(lambda e: body(e, "G"))
        block.sync(lambda e: body(e, "Y"))



def alloc_ln_bufs(nc, es, tag="", reuse=None):
    b = {}
    b["ln_sq"] = es.enter_context(nc.sbuf_tensor(f"lnsq{tag}", [128, 512], F32))
    b["ln_stat"] = es.enter_context(nc.sbuf_tensor(f"lnst{tag}", [1, 1024], F32))
    b["ln_tmp"] = es.enter_context(nc.sbuf_tensor(f"lntmp{tag}", [128, 512], F32))
    b["ones_col"] = es.enter_context(nc.sbuf_tensor(f"lnoc{tag}", [128, 1], F32))
    b["ones1"] = es.enter_context(nc.sbuf_tensor(f"lno1{tag}", [1, 128], F32))
    if reuse is None:
        b["ln_psS"] = es.enter_context(nc.psum_tensor(f"lnpsS{tag}", [1, 512], F32))[:]
        b["ln_psS2"] = es.enter_context(nc.psum_tensor(f"lnpsS2{tag}", [1, 512], F32))[:]
        b["ln_psB"] = es.enter_context(nc.psum_tensor(f"lnpsB{tag}", [128, 512], F32))[:]
        b["ln_psB2"] = es.enter_context(nc.psum_tensor(f"lnpsB2{tag}", [128, 512], F32))[:]
    else:
        b["ln_psS"] = reuse["psS"]
        b["ln_psS2"] = reuse["psS2"]
        b["ln_psB"] = reuse["psB"]
        b["ln_psB2"] = reuse["psB2"]
    return b


def emit_ln_init(p, b):
    t1 = p.op("V", lambda v: v.memset(b["ones_col"][:], 1.0))
    t2 = p.op("V", lambda v: v.memset(b["ones1"][:], 1.0))
    return [t1, t2]


def emit_ln(p, b, x_ap_fn, n_tok, out_fn, gamma_scale=None, gamma_bias=None,
            eps=1e-5, pre_waits=(), extra_out_fn=None, init_toks=()):
    sq, stat, tmp = b["ln_sq"], b["ln_stat"], b["ln_tmp"]
    psS, psS2, psB, psB2 = b["ln_psS"], b["ln_psS2"], b["ln_psB"], b["ln_psB2"]
    ones_col, ones1 = b["ones_col"], b["ones1"]
    n_tt = n_tok // 512
    out_toks = {}
    prev_apply = []
    prev_stats = []
    last = None
    for tt in range(n_tt):
        mu = stat[:, 0:512]
        va = stat[:, 512:1024]
        s_tok = None
        for c in range(NCH):
            w = list(pre_waits) + list(init_toks) + prev_stats if c == 0 else []
            s_tok = p.op("T", (lambda t, c=c, tt=tt: t.matmul(
                psS[:], ones_col[:], x_ap_fn(c, tt), start=(c == 0), stop=(c == NCH - 1))), w)
        prev_mt = None
        for c in range(NCH):
            w = (list(pre_waits) + prev_stats) if c == 0 else [prev_mt]
            st = p.op("S", (lambda s, c=c, tt=tt: s.activation(sq[:], x_ap_fn(c, tt), AF.Square)), w)
            prev_mt = p.op("T", (lambda t, c=c: t.matmul(
                psS2[:], ones_col[:], sq[:], start=(c == 0), stop=(c == NCH - 1))), [st])
        t1 = p.op("V", (lambda v, mu=mu: v.tensor_scalar_mul(mu, psS, 1.0 / D)), [s_tok])
        t2 = p.op("V", (lambda v, va=va: v.tensor_scalar_mul(va, psS2, 1.0 / D)), [prev_mt])
        t3 = p.op("V", (lambda v, mu=mu: v.tensor_mul(sq[0:1, 0:512], mu, mu)), [t1, t2])
        t4 = p.op("V", (lambda v, va=va: v.tensor_sub(va, va, sq[0:1, 0:512])), [t3])
        t4b = p.op("V", (lambda v, va=va: v.tensor_scalar_add(va, va, float(eps))), [t4])
        t5 = p.op("S", (lambda s, va=va: s.activation(va, va, AF.Sqrt)), [t4b])
        t6 = p.op("V", (lambda v, va=va: v.reciprocal(va, va)), [t5])
        b1 = p.op("T", (lambda t, mu=mu: t.matmul(psB, ones1[:], mu, start=True, stop=True)),
                  [t6] + prev_apply)
        b2 = p.op("T", (lambda t, va=va: t.matmul(psB2, ones1[:], va, start=True, stop=True)), [b1])
        prev_stats = [t3, t6]
        prev_apply = []
        for c in range(NCH):
            tA = p.op("V", (lambda v, c=c, tt=tt: v.tensor_sub(tmp[:], x_ap_fn(c, tt), psB)),
                      [b2] + ([prev_apply[-1]] if prev_apply else []))
            if gamma_scale is None:
                tB = p.op("V", (lambda v, c=c, tt=tt: v.tensor_mul(out_fn(c, tt), tmp[:], psB2)), [tA])
            else:
                tB0 = p.op("V", (lambda v: v.tensor_mul(tmp[:], tmp[:], psB2)), [tA])
                tB = p.op("V", (lambda v, c=c, tt=tt: v.tensor_scalar(
                    out_fn(c, tt), tmp[:], gamma_scale(c), gamma_bias(c), ALU.mult, ALU.add)), [tB0])
            if extra_out_fn is not None:
                tB = p.op("S", (lambda s, c=c, tt=tt: s.activation(
                    extra_out_fn(c, tt), out_fn(c, tt), AF.Copy)), [tB])
            out_toks[(c, tt)] = tB
            prev_apply.append(tB)
            last = tB
        prev_stats = prev_stats + [b1, b2]
    return out_toks, last



def build_qkv_launch():
    nc = bass.Bass("TRN2", target_bir_lowering=False)
    xT = nc.dram_tensor("xT", [D, TPC], F32, kind="ExternalInput").ap()
    wq = nc.dram_tensor("wq", [D, D], F32, kind="ExternalInput").ap()
    wk = nc.dram_tensor("wk", [D, D], F32, kind="ExternalInput").ap()
    wv = nc.dram_tensor("wv", [D, D], F32, kind="ExternalInput").ap()
    bq = nc.dram_tensor("bq", [128, NCH], F32, kind="ExternalInput").ap()
    bk = nc.dram_tensor("bk", [128, NCH], F32, kind="ExternalInput").ap()
    bvrow = nc.dram_tensor("bvrow", [1, D], F32, kind="ExternalInput").ap()
    qx = nc.dram_tensor("qx", [D, TPC], BF16, kind="ExternalOutput").ap()
    kx = nc.dram_tensor("kx", [D, TPC], BF16, kind="ExternalOutput").ap()
    vx = nc.dram_tensor("vx", [TPC, D], BF16, kind="ExternalOutput").ap()
    es = ExitStack()
    xbf = es.enter_context(nc.sbuf_tensor("xbf", [128, NCH * TPC], BF16))
    wqb = es.enter_context(nc.sbuf_tensor("wqb", [128, NCH * D], BF16))
    wkb = es.enter_context(nc.sbuf_tensor("wkb", [128, NCH * D], BF16))
    wvb = es.enter_context(nc.sbuf_tensor("wvb", [128, NCH * D], BF16))
    bqs = es.enter_context(nc.sbuf_tensor("bqs", [128, NCH], F32))
    bks = es.enter_context(nc.sbuf_tensor("bks", [128, NCH], F32))
    bvr = es.enter_context(nc.sbuf_tensor("bvr", [1, D], F32))
    bvb = es.enter_context(nc.sbuf_tensor("bvb", [128, D], F32))
    ones1 = es.enter_context(nc.sbuf_tensor("ones1", [1, 128], F32))
    osb = es.enter_context(nc.sbuf_tensor("osb", [128, 2 * TPC], BF16))
    vsb = es.enter_context(nc.sbuf_tensor("vsb", [128, 2 * D], BF16))
    ps = [es.enter_context(nc.psum_tensor(f"ps{i}", [128, 512], F32)) for i in range(2)]
    psv = [es.enter_context(nc.psum_tensor(f"psv{i}", [128, 512], F32)) for i in range(2)]
    psb = es.enter_context(nc.psum_tensor("psb", [128, 512], F32))
    p = Prog(nc, es)

    ldx = p.dma("G", xbf[:].rearrange("p (c t) -> p c t", t=TPC),
                xT.rearrange("(c p) t -> p c t", p=128), group="ldx")
    ldwq = p.dma("G", wqb[:].rearrange("p (c o) -> p c o", o=D), wq.rearrange("(c p) o -> p c o", p=128), group="wq")
    ldwk = p.dma("G", wkb[:].rearrange("p (c o) -> p c o", o=D), wk.rearrange("(c p) o -> p c o", p=128), group="wk")
    ldwv = p.dma("G", wvb[:].rearrange("p (c o) -> p c o", o=D), wv.rearrange("(c p) o -> p c o", p=128), group="wv")
    ldbq = p.dma("Y", bqs[:], bq, group="bq")
    ldbk = p.dma("Y", bks[:], bk, group="bk")
    ldbv = p.dma("Y", bvr[:], bvrow, group="bv")
    onet = p.op("V", lambda v: v.memset(ones1[:], 1.0))
    prevc = onet
    bvb_toks = []
    for h in range(2):
        tb = p.op("T", (lambda t, h=h: t.matmul(psb[:], ones1[:], bvr[:, h * 512:(h + 1) * 512],
                                                start=True, stop=True)), [prevc, ldbv])
        prevc = p.op("V", (lambda v, h=h: v.tensor_copy(bvb[:, h * 512:(h + 1) * 512], psb[:])), [tb])
        bvb_toks.append(prevc)

    bank_free = [None, None]
    half_free = [None, None]
    for name, wb, bsbuf, ldwt, ldbt, out_dram in (
            ("q", wqb, bqs, ldwq, ldbq, qx), ("k", wkb, bks, ldwk, ldbk, kx)):
        for oc in range(NCH):
            half = oc % 2
            st_toks = []
            for tt in range(NTT):
                bi = tt % 2
                pt = ps[bi]
                w0 = [ldx, ldwt, ldbt, bank_free[bi], half_free[half]]
                mm = None
                for ic in range(NCH):
                    mm = p.op("T", (lambda t, ic=ic, oc=oc, tt=tt, pt=pt, wb=wb: t.matmul(
                        pt[:], wb[:, ic * D + oc * 128: ic * D + oc * 128 + 128],
                        xbf[:, ic * TPC + tt * 512: ic * TPC + (tt + 1) * 512],
                        start=(ic == 0), stop=(ic == NCH - 1))), w0 if ic == 0 else [])
                    w0 = []
                ev = p.op("V", (lambda v, tt=tt, pt=pt, oc=oc, half=half, bsbuf=bsbuf: v.tensor_scalar_add(
                    osb[:, half * TPC + tt * 512: half * TPC + (tt + 1) * 512], pt[:],
                    bsbuf[:, oc:oc + 1])), [mm])
                bank_free[bi] = ev
                st_toks.append(ev)
            dt = p.dma("G", out_dram.rearrange("(c p) t -> p c t", p=128)[:, oc, :],
                       osb[:, half * TPC: half * TPC + TPC], waits=st_toks,
                       group=f"oh{half}_{name}")
            half_free[half] = dt

    vbank_free = [None, None]
    vhalf_free = None
    for tc in range(TPC // 128):
        st_toks = []
        for half in range(2):
            pt = psv[half]
            w0 = [ldx, ldwv, vbank_free[half], vhalf_free]
            mm = None
            for ic in range(NCH):
                mm = p.op("T", (lambda t, ic=ic, tc=tc, half=half, pt=pt: t.matmul(
                    pt[:], xbf[:, ic * TPC + tc * 128: ic * TPC + (tc + 1) * 128],
                    wvb[:, ic * D + half * 512: ic * D + (half + 1) * 512],
                    start=(ic == 0), stop=(ic == NCH - 1))), w0 if ic == 0 else [])
                w0 = []
            ev = p.op("V", (lambda v, half=half, pt=pt: v.tensor_add(
                vsb[:, half * D // 2 + 0: 0] if False else vsb[:, half * 512:(half + 1) * 512],
                pt[:], bvb[:, half * 512:(half + 1) * 512])), [mm] + bvb_toks)
            vbank_free[half] = ev
            st_toks.append(ev)
        vhalf_free = p.dma("G", vx[tc * 128:(tc + 1) * 128, :], vsb[:, 0:D], waits=st_toks,
                           group="vout")

    p.emit()
    es.close()
    return nc



def build_flash_dense():
    nc = bass.Bass("TRN2", target_bir_lowering=False)
    q4 = nc.dram_tensor("q4", [4, 64, T], BF16, kind="ExternalInput").ap()
    k4 = nc.dram_tensor("k4", [4, 64, T], BF16, kind="ExternalInput").ap()
    v4 = nc.dram_tensor("v4", [4, T, 64], BF16, kind="ExternalInput").ap()
    ident = nc.dram_tensor("ident", [128, 128], F32, kind="ExternalInput").ap()
    trimask = nc.dram_tensor("trimask", [4, 128, 512], BF16, kind="ExternalInput").ap()
    o4 = nc.dram_tensor("o4", [4, 64, T], BF16, kind="ExternalOutput").ap()
    es = ExitStack()
    SC = 0.125
    qe = es.enter_context(nc.sbuf_tensor("qe", [65, T], BF16))
    ke = es.enter_context(nc.sbuf_tensor("ke", [65, T], BF16))
    ve = es.enter_context(nc.sbuf_tensor("ve", [128, 65 * (T // 128)], BF16))
    idn = es.enter_context(nc.sbuf_tensor("idn", [128, 128], F32))
    tri = es.enter_context(nc.sbuf_tensor("tri", [128, 4 * 512], BF16))
    mneg = es.enter_context(nc.sbuf_tensor("mneg", [128, 4], F32))
    mtmp = es.enter_context(nc.sbuf_tensor("mtmp", [128, 1], F32))
    pt_sb = es.enter_context(nc.sbuf_tensor("pt_sb", [128, 2048], BF16))
    rb = es.enter_context(nc.sbuf_tensor("rb", [64, 512], F32))
    rec = es.enter_context(nc.sbuf_tensor("rec", [1, 512], F32))
    ones1 = es.enter_context(nc.sbuf_tensor("ones1", [1, 64], F32))
    ob = es.enter_context(nc.sbuf_tensor("ob", [64, 2 * 512], BF16))
    psA = es.enter_context(nc.psum_tensor("psA", [128, 512], F32))
    psM = es.enter_context(nc.psum_tensor("psM", [1, 128], F32))
    psT = es.enter_context(nc.psum_tensor("psT", [128, 2048], F32))
    psO = es.enter_context(nc.psum_tensor("psO", [65, 512], F32))
    psR = es.enter_context(nc.psum_tensor("psR", [64, 512], F32))
    p = Prog(nc, es)

    ldi = p.dma("Y", idn[:], ident, group="ldi")
    ldt = p.dma("Y", tri[:].rearrange("p (s n) -> p s n", n=512), trimask.rearrange("s p n -> p s n"),
                group="ldt")
    one1 = p.op("V", lambda v: v.memset(ones1[:], 1.0))
    onek = p.op("V", lambda v: v.memset(ke[64:65, :], 1.0))
    onev = p.op("V", lambda v: v.memset(ve[:].rearrange("p (c e) -> p c e", e=65)[:, :, 64], 1.0))

    NQG = T // 512
    prev_pair_done = [onek, onev]
    prev_cpm = None
    ob_last = [None, None]
    for pair in range(4):
        ldq = p.dma("G", qe[0:64, :], q4[pair], waits=prev_pair_done, group="ldq")
        ldk = p.dma("G", ke[0:64, :], k4[pair], waits=prev_pair_done, group="ldk")
        ldv = p.dma("G", ve[:].rearrange("p (c e) -> p c e", e=65)[:, :, 0:64],
                    v4[pair].rearrange("(c p) d -> p c d", p=128), waits=prev_pair_done,
                    group="ldv")
        prev_ot = None
        for g in range(NQG):
            nkc = g + 1
            tr_toks = []
            for qt in range(4):
                qsl = qe[0:64, g * 512 + qt * 128: g * 512 + qt * 128 + 128]
                red_prev = None
                for kc in range(nkc):
                    w = [ldq, ldk] if (qt == 0 and kc == 0) else []
                    w = w + ([red_prev] if red_prev else [])
                    mm = p.op("T", (lambda t, qsl=qsl, kc=kc: t.matmul(
                        psA[:], qsl, ke[0:64, kc * 512:(kc + 1) * 512], start=True, stop=True)), w)
                    if kc == 0:
                        red_prev = p.op("V", (lambda v, qt=qt: v.reduce_max(
                            mneg[:, qt:qt + 1], psA[:], axis=AX.X, negate=True)), [mm])
                    else:
                        r1 = p.op("V", (lambda v: v.reduce_max(mtmp[:], psA[:], axis=AX.X,
                                                               negate=True)), [mm])
                        red_prev = p.op("V", (lambda v, qt=qt: v.tensor_tensor(
                            mneg[:, qt:qt + 1], mneg[:, qt:qt + 1], mtmp[:], ALU.min)), [r1])
                trm = p.op("T", (lambda t, qt=qt: t.transpose(psM[:], mneg[:, qt:qt + 1], idn[:])),
                           [red_prev, ldi, prev_cpm])
                cpm = p.op("V", (lambda v, g=g, qt=qt: v.tensor_copy(
                    qe[64:65, g * 512 + qt * 128: g * 512 + qt * 128 + 128], psM[:])), [trm])
                prev_cpm = cpm
                tr_toks.append(cpm)
            qg = qe[:, g * 512:(g + 1) * 512]
            first_st_waits = tr_toks + [prev_ot]
            ex_prev = None
            avm = None
            for kc in range(nkc):
                mm4 = None
                for s in range(4):
                    w = (first_st_waits + [ex_prev]) if s == 0 else []
                    mm4 = p.op("T", (lambda t, kc=kc, s=s, qg=qg: t.matmul(
                        psT[:, s * 512:(s + 1) * 512],
                        ke[:, kc * 512 + s * 128: kc * 512 + s * 128 + 128], qg,
                        start=True, stop=True)), w)
                first_st_waits = []
                ex = p.op("S", (lambda s_: s_.activation(pt_sb[:], psT[:], AF.Exp, scale=SC)),
                          [mm4, avm])
                ex_prev = ex
                dep = [ex]
                if kc == g:
                    for s in range(4):
                        dep = [p.op("V", (lambda v, s=s: v.tensor_mul(
                            pt_sb[:, s * 512:(s + 1) * 512], pt_sb[:, s * 512:(s + 1) * 512],
                            tri[:, s * 512:(s + 1) * 512])), dep + ([ldt] if s == 0 else []))]
                for s in range(4):
                    w = (dep + [ldv]) if s == 0 else []
                    avm = p.op("T", (lambda t, kc=kc, s=s, g=g: t.matmul(
                        psO[:], ve[:, (kc * 4 + s) * 65:(kc * 4 + s) * 65 + 65],
                        pt_sb[:, s * 512:(s + 1) * 512],
                        start=(kc == 0 and s == 0), stop=(kc == nkc - 1 and s == 3))), w)
            rc = p.op("V", (lambda v: v.reciprocal(rec[:], psO[64:65, :])), [avm])
            bc = p.op("T", (lambda t: t.matmul(psR[:], ones1[:], rec[:], start=True, stop=True)),
                      [rc, one1])
            cb = p.op("V", (lambda v: v.tensor_copy(rb[:], psR[:])), [bc])
            ot = p.op("V", (lambda v, g=g: v.tensor_mul(
                ob[:, (g % 2) * 512:(g % 2) * 512 + 512], psO[0:64, :], rb[:])),
                [cb, ob_last[g % 2]])
            ob_last[g % 2] = p.dma("G", o4[pair][:, g * 512:(g + 1) * 512],
                                   ob[:, (g % 2) * 512:(g % 2) * 512 + 512], waits=[ot],
                                   group=f"ob{g % 2}")
            prev_ot = ot
        prev_pair_done = [prev_ot, ob_last[0], ob_last[1]]
    p.emit()
    es.close()
    return nc



def build_proj_ln():
    nc = bass.Bass("TRN2", target_bir_lowering=False)
    oT = nc.dram_tensor("oT", [D, TPC], BF16, kind="ExternalInput").ap()
    wo = nc.dram_tensor("wo", [D, D], F32, kind="ExternalInput").ap()
    bo = nc.dram_tensor("bo", [128, NCH], F32, kind="ExternalInput").ap()
    xT = nc.dram_tensor("xT", [D, TPC], F32, kind="ExternalInput").ap()
    xn = nc.dram_tensor("xn", [D, TPC], F32, kind="ExternalOutput").ap()
    es = ExitStack()
    obf = es.enter_context(nc.sbuf_tensor("obf", [128, NCH * TPC], BF16))
    wob = es.enter_context(nc.sbuf_tensor("wob", [128, NCH * D], BF16))
    bos = es.enter_context(nc.sbuf_tensor("bos", [128, NCH], F32))
    xr = es.enter_context(nc.sbuf_tensor("xr", [128, NCH * TPC], F32))
    xo = es.enter_context(nc.sbuf_tensor("xo", [128, 512], F32))
    ps = [es.enter_context(nc.psum_tensor(f"ps{i}", [128, 512], F32)) for i in range(2)]
    lnb = alloc_ln_bufs(nc, es)
    p = Prog(nc, es)

    ldo = p.dma("G", obf[:].rearrange("p (c t) -> p c t", t=TPC),
                oT.rearrange("(c p) t -> p c t", p=128), group="ldo")
    ldw = p.dma("G", wob[:].rearrange("p (c o) -> p c o", o=D), wo.rearrange("(c p) o -> p c o", p=128), group="ldw")
    ldb = p.dma("Y", bos[:], bo, group="ldb")
    ldxr = p.dma("G", xr[:].rearrange("p (c t) -> p c t", t=TPC),
                 xT.rearrange("(c p) t -> p c t", p=128), group="ldxr")
    init = emit_ln_init(p, lnb)

    bank_free = [None, None]
    res_toks = []
    for oc in range(NCH):
        for tt in range(NTT):
            bi = (oc * NTT + tt) % 2
            pt = ps[bi]
            w0 = [ldo, ldw, ldb, ldxr, bank_free[bi]]
            mm = None
            for ic in range(NCH):
                mm = p.op("T", (lambda t, ic=ic, oc=oc, tt=tt, pt=pt: t.matmul(
                    pt[:], wob[:, ic * D + oc * 128: ic * D + oc * 128 + 128],
                    obf[:, ic * TPC + tt * 512: ic * TPC + (tt + 1) * 512],
                    start=(ic == 0), stop=(ic == NCH - 1))), w0 if ic == 0 else [])
                w0 = []
            ev = p.op("V", (lambda v, pt=pt, oc=oc: v.tensor_scalar_add(
                xo[:], pt[:], bos[:, oc:oc + 1])), [mm])
            bank_free[bi] = ev
            ad = p.op("V", (lambda v, oc=oc, tt=tt: v.tensor_add(
                xr[:, oc * TPC + tt * 512: oc * TPC + (tt + 1) * 512],
                xr[:, oc * TPC + tt * 512: oc * TPC + (tt + 1) * 512], xo[:])), [ev])
            bank_free[bi] = ad
            res_toks.append(ad)

    def x_ap(c, tt):
        return xr[:, c * TPC + tt * 512: c * TPC + (tt + 1) * 512]

    toks, last = emit_ln(p, lnb, x_ap, TPC, out_fn=x_ap, pre_waits=res_toks, init_toks=init)
    for c in range(NCH):
        for tt in range(NTT):
            p.dma("G", xn.rearrange("(c p) t -> p c t", p=128)[:, c, tt * 512:(tt + 1) * 512],
                  x_ap(c, tt), waits=[toks[(c, tt)]], group="xout")
    p.emit()
    es.close()
    return nc




INNER = 512
NQC = 4


def build_lffn_qkv():
    nc = bass.Bass("TRN2", target_bir_lowering=False)
    xT = nc.dram_tensor("xT", [D, TPC], F32, kind="ExternalInput").ap()
    xprevT = nc.dram_tensor("xprevT", [D, 1], F32, kind="ExternalInput").ap()
    wqkv = nc.dram_tensor("wqkv", [D, 3 * INNER], F32, kind="ExternalInput").ap()
    qs2 = nc.dram_tensor("qs2", [128, 1], F32, kind="ExternalInput").ap()
    ks2 = nc.dram_tensor("ks2", [128, 1], F32, kind="ExternalInput").ap()
    tabs = {n: nc.dram_tensor(n, [128, TPC], F32, kind="ExternalInput").ap()
            for n in ("AQ", "BQ", "AK", "BK")}
    E2 = nc.dram_tensor("E2", [128, 128], F32, kind="ExternalInput").ap()
    qx = nc.dram_tensor("qx", [INNER, TPC], BF16, kind="ExternalOutput").ap()
    kx = nc.dram_tensor("kx", [INNER, TPC], BF16, kind="ExternalOutput").ap()
    vx = nc.dram_tensor("vx", [TPC, INNER], BF16, kind="ExternalOutput").ap()
    es = ExitStack()
    W = 2049
    xbf = es.enter_context(nc.sbuf_tensor("xbf", [128, NCH * W], BF16))
    wb = es.enter_context(nc.sbuf_tensor("wb", [128, NCH * 3 * INNER], BF16))
    qss = es.enter_context(nc.sbuf_tensor("qss", [128, 1], F32))
    kss = es.enter_context(nc.sbuf_tensor("kss", [128, 1], F32))
    tb = {n: es.enter_context(nc.sbuf_tensor("t" + n, [128, TPC], F32)) for n in ("AQ", "BQ", "AK", "BK")}
    e2s = es.enter_context(nc.sbuf_tensor("e2s", [128, 128], F32))
    sqb = es.enter_context(nc.sbuf_tensor("sqb", [128, 512], F32))
    rstd = es.enter_context(nc.sbuf_tensor("rstd", [128, 512], F32))
    qsq = es.enter_context(nc.sbuf_tensor("qsq", [128, 512], F32))
    qrot = es.enter_context(nc.sbuf_tensor("qrot", [128, 512], F32))
    u = es.enter_context(nc.sbuf_tensor("u", [128, 512], F32))
    w_ = es.enter_context(nc.sbuf_tensor("w_", [128, 512], F32))
    osb = es.enter_context(nc.sbuf_tensor("osb", [128, 2 * TPC], BF16))
    vsb = es.enter_context(nc.sbuf_tensor("vsb", [128, 512], BF16))
    pQ = [es.enter_context(nc.psum_tensor(f"pQ{i}", [128, 512], F32)) for i in range(2)]
    pS = es.enter_context(nc.psum_tensor("pS", [128, 512], F32))
    pV = [es.enter_context(nc.psum_tensor(f"pV{i}", [128, 512], F32)) for i in range(2)]
    p = Prog(nc, es)

    ldx = p.dma("G", xbf[:].rearrange("p (c t) -> p c t", t=W)[:, :, 1:W],
                xT.rearrange("(c p) t -> p c t", p=128), group="ldx")
    ldxp = p.dma("G", xbf[:].rearrange("p (c t) -> p c t", t=W)[:, :, 0:1],
                 xprevT.rearrange("(c p) t -> p c t", p=128), allow_slow_non_contiguous=True,
                 group="ldxp")
    ldw = p.dma("G", wb[:].rearrange("p (c o) -> p c o", o=3 * INNER),
                wqkv.rearrange("(c p) o -> p c o", p=128), group="ldw")
    ldqs = p.dma("Y", qss[:], qs2, group="ldqs")
    ldks = p.dma("Y", kss[:], ks2, group="ldks")
    ldtb = [p.dma("Y", tb[n][:], tabs[n], group="t" + n) for n in ("AQ", "BQ", "AK", "BK")]
    lde2 = p.dma("Y", e2s[:], E2, group="lde2")
    lds = [ldx, ldxp, ldw, ldqs, ldks, lde2] + ldtb

    def h_ap(ic, t0, n):
        off = 0 if ic >= 4 else 1
        return xbf[:, ic * W + off + t0: ic * W + off + t0 + n]

    Q0, Q1, Q2, Q3 = (slice(0, 32), slice(32, 64), slice(64, 96), slice(96, 128))
    QPART = [(Q0, Q1), (Q1, Q0), (Q2, Q3), (Q3, Q2)]

    bank_free = [None, None]
    half_free = [None, None]
    prev_t3 = None
    for name, qkoff, ssb, A, Bt, out_dram in (
            ("q", 0, qss, "AQ", "BQ", qx), ("k", 4, kss, "AK", "BK", kx)):
        for oc in range(NQC):
            half = oc % 2
            st_toks = []
            for tt in range(NTT):
                bi = tt % 2
                pt = pQ[bi]
                w0 = lds + [bank_free[bi], half_free[half]]
                mm = None
                for ic in range(NCH):
                    mm = p.op("T", (lambda t, ic=ic, oc=oc, tt=tt, pt=pt, qkoff=qkoff: t.matmul(
                        pt[:], wb[:, ic * 3 * INNER + (qkoff + oc) * 128: ic * 3 * INNER + (qkoff + oc) * 128 + 128],
                        h_ap(ic, tt * 512, 512),
                        start=(ic == 0), stop=(ic == NCH - 1))), w0 if ic == 0 else [])
                    w0 = []
                sg = p.op("S", (lambda s, pt=pt: s.activation(sqb[:], pt[:], AF.Square)), [mm])
                sm = p.op("T", (lambda t: t.matmul(pS[:], e2s[:], sqb[:], start=True, stop=True)), [sg])
                nr = p.op("S", (lambda s: s.activation(rstd[:], pS[:], AF.Sqrt)), [sm, prev_t3])
                nr2 = p.op("V", (lambda v: v.tensor_scalar_max(rstd[:], rstd[:], 1e-12)), [nr])
                nr3 = p.op("V", (lambda v: v.reciprocal(rstd[:], rstd[:])), [nr2])
                ev = p.op("V", (lambda v, pt=pt, ssb=ssb: v.tensor_scalar_mul(qsq[:], pt[:], ssb[:])), [mm])
                bank_free[bi] = ev
                t1 = p.op("V", (lambda v, tt=tt, A=A: v.tensor_mul(
                    u[:], qsq[:], tb[A][:, tt * 512:(tt + 1) * 512])), [ev])
                rt = None
                for (dst, srcq) in QPART:
                    rt = p.op("V", (lambda v, dst=dst, srcq=srcq: v.tensor_copy(
                        qrot[dst, :], qsq[srcq, :])), [ev])
                rt = p.op("V", (lambda v, tt=tt, Bt=Bt: v.tensor_mul(
                    w_[:], qrot[:], tb[Bt][:, tt * 512:(tt + 1) * 512])), [rt])
                t2 = p.op("V", (lambda v: v.tensor_add(u[:], u[:], w_[:])), [t1, rt])
                t3 = p.op("V", (lambda v, oc=oc, tt=tt, half=half: v.tensor_mul(
                    osb[:, half * TPC + tt * 512: half * TPC + (tt + 1) * 512], u[:], rstd[:])),
                    [t2, nr3])
                prev_t3 = t3
                st_toks.append(t3)
            dt = p.dma("G", out_dram.rearrange("(c p) t -> p c t", p=128)[:, oc, :],
                       osb[:, half * TPC: half * TPC + TPC], waits=st_toks,
                       group=f"oh{half}_{name}")
            half_free[half] = dt

    vbank_free = [None, None]
    vhalf_free = None
    for tc in range(TPC // 128):
        bi = tc % 2
        pt = pV[bi]
        w0 = lds + [vbank_free[bi], vhalf_free]
        mm = None
        for ic in range(NCH):
            mm = p.op("T", (lambda t, ic=ic, tc=tc, pt=pt: t.matmul(
                pt[:], h_ap(ic, tc * 128, 128),
                wb[:, ic * 3 * INNER + 2 * INNER: ic * 3 * INNER + 3 * INNER],
                start=(ic == 0), stop=(ic == NCH - 1))), w0 if ic == 0 else [])
            w0 = []
        ev = p.op("V", (lambda v, pt=pt: v.tensor_copy(vsb[:], pt[:])), [mm])
        vbank_free[bi] = ev
        vhalf_free = p.dma("G", vx[tc * 128:(tc + 1) * 128, :], vsb[:], waits=[ev], group="vout")

    p.emit()
    es.close()
    return nc



def build_lffn_attn():
    nc = bass.Bass("TRN2", target_bir_lowering=False)
    KL = TPC + 512
    qx = nc.dram_tensor("qx", [INNER, TPC], BF16, kind="ExternalInput").ap()
    kfb = nc.dram_tensor("kfb", [INNER, KL], BF16, kind="ExternalInput").ap()
    vfb = nc.dram_tensor("vfb", [KL, INNER], BF16, kind="ExternalInput").ap()
    masks = nc.dram_tensor("masks", [32, 128, 512], BF16, kind="ExternalInput").ap()
    ident = nc.dram_tensor("ident", [128, 128], F32, kind="ExternalInput").ap()
    ox = nc.dram_tensor("ox", [INNER, TPC], BF16, kind="ExternalOutput").ap()
    es = ExitStack()
    qe = es.enter_context(nc.sbuf_tensor("qe", [65, TPC], BF16))
    ke = es.enter_context(nc.sbuf_tensor("ke", [65, KL], BF16))
    ve = es.enter_context(nc.sbuf_tensor("ve", [128, 65 * (KL // 128)], BF16))
    msk = es.enter_context(nc.sbuf_tensor("msk", [128, 32 * 512], BF16))
    idn = es.enter_context(nc.sbuf_tensor("idn", [128, 128], F32))
    mneg = es.enter_context(nc.sbuf_tensor("mneg", [128, 4], F32))
    mtmp = es.enter_context(nc.sbuf_tensor("mtmp", [128, 1], F32))
    pt_sb = es.enter_context(nc.sbuf_tensor("pt_sb", [128, 2048], BF16))
    rb = es.enter_context(nc.sbuf_tensor("rb", [64, 512], F32))
    rec = es.enter_context(nc.sbuf_tensor("rec", [1, 512], F32))
    ones1 = es.enter_context(nc.sbuf_tensor("ones1", [1, 64], F32))
    ob = es.enter_context(nc.sbuf_tensor("ob", [64, 2 * 512], BF16))
    psA = es.enter_context(nc.psum_tensor("psA", [128, 512], F32))
    psM = es.enter_context(nc.psum_tensor("psM", [1, 128], F32))
    psT = es.enter_context(nc.psum_tensor("psT", [128, 2048], F32))
    psO = es.enter_context(nc.psum_tensor("psO", [65, 512], F32))
    psR = es.enter_context(nc.psum_tensor("psR", [64, 512], F32))
    p = Prog(nc, es)

    ldi = p.dma("Y", idn[:], ident, group="ldi")
    ldm = p.dma("Y", msk[:].rearrange("p (s n) -> p s n", n=512), masks.rearrange("s p n -> p s n"),
                group="ldm")
    one1 = p.op("V", lambda v: v.memset(ones1[:], 1.0))
    onek = p.op("V", lambda v: v.memset(ke[64:65, :], 1.0))
    onev = p.op("V", lambda v: v.memset(ve[:].rearrange("p (c e) -> p c e", e=65)[:, :, 64], 1.0))
    prev_head_done = [onek, onev]
    prev_cpm = None
    ob_last = [None, None]
    for h in range(8):
        ldq = p.dma("G", qe[0:64, :], qx[h * 64:(h + 1) * 64, :], waits=prev_head_done, group="ldq")
        ldk = p.dma("G", ke[0:64, :], kfb[h * 64:(h + 1) * 64, :], waits=prev_head_done, group="ldk")
        ldv = p.dma("G", ve[:].rearrange("p (c e) -> p c e", e=65)[:, :, 0:64],
                    vfb.rearrange("(c p) d -> p c d", p=128)[:, :, h * 64:(h + 1) * 64],
                    waits=prev_head_done, group="ldv")
        prev_ot = None
        for wi in range(4):
            tr_toks = []
            for qt in range(4):
                qsl = qe[0:64, wi * 512 + qt * 128: wi * 512 + qt * 128 + 128]
                red_prev = None
                for kc in range(2):
                    w = [ldq, ldk] if (qt == 0 and kc == 0) else []
                    w = w + ([red_prev] if red_prev else [])
                    mm = p.op("T", (lambda t, qsl=qsl, kc=kc, wi=wi: t.matmul(
                        psA[:], qsl, ke[0:64, wi * 512 + kc * 512: wi * 512 + (kc + 1) * 512],
                        start=True, stop=True)), w)
                    if kc == 0:
                        red_prev = p.op("V", (lambda v, qt=qt: v.reduce_max(
                            mneg[:, qt:qt + 1], psA[:], axis=AX.X, negate=True)), [mm])
                    else:
                        r1 = p.op("V", (lambda v: v.reduce_max(mtmp[:], psA[:], axis=AX.X,
                                                               negate=True)), [mm])
                        red_prev = p.op("V", (lambda v, qt=qt: v.tensor_tensor(
                            mneg[:, qt:qt + 1], mneg[:, qt:qt + 1], mtmp[:], ALU.min)), [r1])
                trm = p.op("T", (lambda t, qt=qt: t.transpose(psM[:], mneg[:, qt:qt + 1], idn[:])),
                           [red_prev, ldi, prev_cpm])
                cpm = p.op("V", (lambda v, wi=wi, qt=qt: v.tensor_copy(
                    qe[64:65, wi * 512 + qt * 128: wi * 512 + qt * 128 + 128], psM[:])), [trm])
                prev_cpm = cpm
                tr_toks.append(cpm)
            qg = qe[:, wi * 512:(wi + 1) * 512]
            first_st_waits = tr_toks + [prev_ot]
            ex_prev = None
            avm = None
            for kc in range(2):
                mm4 = None
                for s in range(4):
                    w = (first_st_waits + [ex_prev]) if s == 0 else []
                    mm4 = p.op("T", (lambda t, kc=kc, s=s, qg=qg, wi=wi: t.matmul(
                        psT[:, s * 512:(s + 1) * 512],
                        ke[:, wi * 512 + kc * 512 + s * 128: wi * 512 + kc * 512 + s * 128 + 128], qg,
                        start=True, stop=True)), w)
                first_st_waits = []
                ex = p.op("S", (lambda s_: s_.activation(pt_sb[:], psT[:], AF.Exp, scale=8.0)),
                          [mm4, avm])
                ex_prev = ex
                dep = [ex]
                for s in range(4):
                    mi = wi * 8 + kc * 4 + s
                    dep = [p.op("V", (lambda v, s=s, mi=mi: v.tensor_mul(
                        pt_sb[:, s * 512:(s + 1) * 512], pt_sb[:, s * 512:(s + 1) * 512],
                        msk[:, mi * 512:(mi + 1) * 512])), dep + ([ldm] if s == 0 else []))]
                for s in range(4):
                    w = (dep + [ldv]) if s == 0 else []
                    avm = p.op("T", (lambda t, kc=kc, s=s, wi=wi: t.matmul(
                        psO[:], ve[:, (wi * 4 + kc * 4 + s) * 65:(wi * 4 + kc * 4 + s) * 65 + 65],
                        pt_sb[:, s * 512:(s + 1) * 512],
                        start=(kc == 0 and s == 0), stop=(kc == 1 and s == 3))), w)
            rc = p.op("V", (lambda v: v.reciprocal(rec[:], psO[64:65, :])), [avm])
            bc = p.op("T", (lambda t: t.matmul(psR[:], ones1[:], rec[:], start=True, stop=True)),
                      [rc, one1])
            cb = p.op("V", (lambda v: v.tensor_copy(rb[:], psR[:])), [bc])
            ot = p.op("V", (lambda v, wi=wi: v.tensor_mul(
                ob[:, (wi % 2) * 512:(wi % 2) * 512 + 512], psO[0:64, :], rb[:])),
                [cb, ob_last[wi % 2]])
            ob_last[wi % 2] = p.dma("G", ox[h * 64:(h + 1) * 64, wi * 512:(wi + 1) * 512],
                                    ob[:, (wi % 2) * 512:(wi % 2) * 512 + 512], waits=[ot],
                                    group=f"ob{wi % 2}")
            prev_ot = ot
        prev_head_done = [prev_ot, ob_last[0], ob_last[1]]
    p.emit()
    es.close()
    return nc



def build_lffn_proj():
    nc = bass.Bass("TRN2", target_bir_lowering=False)
    oT = nc.dram_tensor("oT", [INNER, TPC], BF16, kind="ExternalInput").ap()
    wo = nc.dram_tensor("wo", [INNER, D], F32, kind="ExternalInput").ap()
    xT = nc.dram_tensor("xT", [D, TPC], F32, kind="ExternalInput").ap()
    x2 = nc.dram_tensor("x2", [D, TPC], F32, kind="ExternalOutput").ap()
    es = ExitStack()
    obf = es.enter_context(nc.sbuf_tensor("obf", [128, 4 * TPC], BF16))
    wob = es.enter_context(nc.sbuf_tensor("wob", [128, 4 * D], BF16))
    xr = es.enter_context(nc.sbuf_tensor("xr", [128, NCH * TPC], F32))
    xo = es.enter_context(nc.sbuf_tensor("xo", [128, 512], F32))
    ps = [es.enter_context(nc.psum_tensor(f"ps{i}", [128, 512], F32)) for i in range(2)]
    p = Prog(nc, es)
    ldo = p.dma("G", obf[:].rearrange("p (c t) -> p c t", t=TPC),
                oT.rearrange("(c p) t -> p c t", p=128), group="ldo")
    ldw = p.dma("G", wob[:].rearrange("p (c o) -> p c o", o=D), wo.rearrange("(c p) o -> p c o", p=128), group="ldw")
    ldxr = p.dma("G", xr[:].rearrange("p (c t) -> p c t", t=TPC),
                 xT.rearrange("(c p) t -> p c t", p=128), group="ldxr")
    bank_free = [None, None]
    for oc in range(NCH):
        for tt in range(NTT):
            bi = (oc * NTT + tt) % 2
            pt = ps[bi]
            w0 = [ldo, ldw, ldxr, bank_free[bi]]
            mm = None
            for ic in range(4):
                mm = p.op("T", (lambda t, ic=ic, oc=oc, tt=tt, pt=pt: t.matmul(
                    pt[:], wob[:, ic * D + oc * 128: ic * D + oc * 128 + 128],
                    obf[:, ic * TPC + tt * 512: ic * TPC + (tt + 1) * 512],
                    start=(ic == 0), stop=(ic == 3))), w0 if ic == 0 else [])
                w0 = []
            ad = p.op("V", (lambda v, oc=oc, tt=tt, pt=pt: v.tensor_add(
                xr[:, oc * TPC + tt * 512: oc * TPC + (tt + 1) * 512],
                xr[:, oc * TPC + tt * 512: oc * TPC + (tt + 1) * 512], pt[:])), [mm])
            bank_free[bi] = ad
            p.dma("G", x2.rearrange("(c p) t -> p c t", p=128)[:, oc, tt * 512:(tt + 1) * 512],
                  xr[:, oc * TPC + tt * 512: oc * TPC + (tt + 1) * 512], waits=[ad], group="xout")
    p.emit()
    es.close()
    return nc



HID_P = 2816


def build_lffn_ffn():
    nc = bass.Bass("TRN2", target_bir_lowering=False)
    x2T = nc.dram_tensor("x2T", [D, TPC], F32, kind="ExternalInput").ap()
    x2prevT = nc.dram_tensor("x2prevT", [D, 1], F32, kind="ExternalInput").ap()
    lg = nc.dram_tensor("lg", [128, NCH], F32, kind="ExternalInput").ap()
    lb = nc.dram_tensor("lb", [128, NCH], F32, kind="ExternalInput").ap()
    w1p = nc.dram_tensor("w1p", [D, 2 * HID_P], F32, kind="ExternalInput").ap()
    b1p = nc.dram_tensor("b1p", [128, 2 * HID_P // 128], F32, kind="ExternalInput").ap()
    w2p = nc.dram_tensor("w2p", [HID_P, D], F32, kind="ExternalInput").ap()
    b2 = nc.dram_tensor("b2", [128, NCH], F32, kind="ExternalInput").ap()
    x3 = nc.dram_tensor("x3", [D, TPC], F32, kind="ExternalOutput").ap()
    es = ExitStack()
    W = 2049
    NA = HID_P // 128
    xsh = es.enter_context(nc.sbuf_tensor("xsh", [128, NCH * W], F32))
    hbf = es.enter_context(nc.sbuf_tensor("hbf", [128, NCH * TPC], BF16))
    lgs = es.enter_context(nc.sbuf_tensor("lgs", [128, NCH], F32))
    lbs = es.enter_context(nc.sbuf_tensor("lbs", [128, NCH], F32))
    b1s = es.enter_context(nc.sbuf_tensor("b1s", [128, 2 * NA], F32))
    b2s = es.enter_context(nc.sbuf_tensor("b2s", [128, NCH], F32))
    w1sb = es.enter_context(nc.sbuf_tensor("w1sb", [128, 2 * 2 * NCH * 128], BF16))
    w2sb = es.enter_context(nc.sbuf_tensor("w2sb", [128, 2 * NA * 128], BF16))
    pbuf = es.enter_context(nc.sbuf_tensor("pbuf", [128, NA * 512], BF16))
    gg = es.enter_context(nc.sbuf_tensor("gg", [128, 512], BF16))
    atmp = es.enter_context(nc.sbuf_tensor("atmp", [128, 512], F32))
    xo = es.enter_context(nc.sbuf_tensor("xo", [128, 512], F32))
    ps = [es.enter_context(nc.psum_tensor(f"ps{i}", [128, 512], F32)) for i in range(2)]
    lnb = alloc_ln_bufs(nc, es)
    p = Prog(nc, es)

    ldx = p.dma("G", xsh[:].rearrange("p (c t) -> p c t", t=W)[:, :, 1:W],
                x2T.rearrange("(c p) t -> p c t", p=128), group="ldx")
    ldxp = p.dma("G", xsh[:].rearrange("p (c t) -> p c t", t=W)[:, :, 0:1],
                 x2prevT.rearrange("(c p) t -> p c t", p=128), allow_slow_non_contiguous=True,
                 group="ldxp")
    ldlg = p.dma("Y", lgs[:], lg, group="ldlg")
    ldlb = p.dma("Y", lbs[:], lb, group="ldlb")
    ldb1 = p.dma("Y", b1s[:], b1p, group="ldb1")
    ldb2 = p.dma("Y", b2s[:], b2, group="ldb2")
    init = emit_ln_init(p, lnb)

    def xs_ap(c, tt):
        off = 0 if c >= 4 else 1
        return xsh[:, c * W + off + tt * 512: c * W + off + tt * 512 + 512]

    def xu_ap(c, tt):
        return xsh[:, c * W + 1 + tt * 512: c * W + 1 + tt * 512 + 512]

    def hb_ap(c, tt):
        return hbf[:, c * TPC + tt * 512: c * TPC + (tt + 1) * 512]

    toks, last_ln = emit_ln(p, lnb, xs_ap, TPC, out_fn=hb_ap,
                            gamma_scale=lambda c: lgs[:, c:c + 1],
                            gamma_bias=lambda c: lbs[:, c:c + 1],
                            pre_waits=[ldx, ldxp, ldlg, ldlb], init_toks=init)
    ln_done = [toks[(c, tt)] for c in range(NCH) for tt in range(NTT)]

    w1r = w1p.rearrange("(c p) o -> p c o", p=128)
    w2r = w2p.rearrange("(c p) o -> p c o", p=128)
    w1v = w1sb[:].rearrange("p (b a c o) -> p b a c o", b=2, a=2, o=128)
    w2v = w2sb[:].rearrange("p (b c o) -> p b c o", b=2, o=128)
    prev_pb_use = None
    last_x3 = None
    w1_free = [None, None]
    w2_free = [None, None]
    gbank = [None, None]
    for tt in range(NTT):
        pb_toks = []
        for ocp in range(NA):
            par = ocp % 2
            lwa = p.dma("G", w1v[:, par, 0], w1r[:, :, ocp * 128:(ocp + 1) * 128],
                        waits=[w1_free[par]], group=f"w1_{par}")
            lwg = p.dma("G", w1v[:, par, 1], w1r[:, :, HID_P + ocp * 128: HID_P + ocp * 128 + 128],
                        group=f"w1_{par}")
            w0 = ln_done + [lwa, lwg, gbank[0], prev_pb_use]
            mmg = None
            for ic in range(NCH):
                mmg = p.op("T", (lambda t, ic=ic, par=par, tt=tt: t.matmul(
                    ps[0][:], w1v[:, par, 1, ic], hb_ap(ic, tt),
                    start=(ic == 0), stop=(ic == NCH - 1))), w0 if ic == 0 else [])
                w0 = []
            ge = p.op("S", (lambda s, ocp=ocp: s.activation(
                gg[:], ps[0][:], AF.Gelu, bias=b1s[:, NA + ocp:NA + ocp + 1])), [mmg, ldb1])
            gbank[0] = ge
            w0 = [gbank[1], ge]
            mma = None
            for ic in range(NCH):
                mma = p.op("T", (lambda t, ic=ic, par=par, tt=tt: t.matmul(
                    ps[1][:], w1v[:, par, 0, ic], hb_ap(ic, tt),
                    start=(ic == 0), stop=(ic == NCH - 1))), w0 if ic == 0 else [])
                w0 = []
            ae = p.op("V", (lambda v, ocp=ocp: v.tensor_scalar_add(
                atmp[:], ps[1][:], b1s[:, ocp:ocp + 1])), [mma, ldb1])
            gbank[1] = ae
            pm = p.op("V", (lambda v, ocp=ocp: v.tensor_mul(
                pbuf[:, ocp * 512:(ocp + 1) * 512], atmp[:], gg[:])), [ae, ge])
            w1_free[par] = pm
            pb_toks.append(pm)
        x3_toks = []
        last_w2mm = None
        for oc2 in range(NCH):
            par = oc2 % 2
            lw2 = p.dma("G", w2v[:, par], w2r[:, :, oc2 * 128:(oc2 + 1) * 128],
                        waits=[w2_free[par]], group=f"w2_{par}")
            w0 = pb_toks + [lw2, gbank[0], gbank[1]]
            mm = None
            for kc in range(NA):
                mm = p.op("T", (lambda t, kc=kc, par=par, tt=tt: t.matmul(
                    ps[0][:], w2v[:, par, kc], pbuf[:, kc * 512:(kc + 1) * 512],
                    start=(kc == 0), stop=(kc == NA - 1))), w0 if kc == 0 else [])
                w0 = []
            last_w2mm = mm
            w2_free[par] = mm
            ev = p.op("V", (lambda v, oc2=oc2: v.tensor_scalar_add(
                xo[:], ps[0][:], b2s[:, oc2:oc2 + 1])), [mm, ldb2])
            gbank[0] = ev
            ad = p.op("V", (lambda v, oc2=oc2, tt=tt: v.tensor_add(
                xo[:], xo[:], xu_ap(oc2, tt))), [ev])
            dt = p.dma("G", x3.rearrange("(c p) t -> p c t", p=128)[:, oc2, tt * 512:(tt + 1) * 512],
                       xo[:], waits=[ad], group="xout")
            gbank[0] = dt
            x3_toks.append(ad)
        prev_pb_use = last_w2mm
        last_x3 = x3_toks[-1]
    p.emit()
    es.close()
    return nc





DFF = 4096
NE = 8


def build_moe():
    nc = bass.Bass("TRN2", target_bir_lowering=False)
    xT = nc.dram_tensor("xT", [D, TPC], F32, kind="ExternalInput").ap()
    rw = nc.dram_tensor("rw", [D, NE], F32, kind="ExternalInput").ap()
    rbrow = nc.dram_tensor("rbrow", [1, NE], F32, kind="ExternalInput").ap()
    upT = nc.dram_tensor("upT", [NE, D, DFF], F32, kind="ExternalInput").ap()
    downT = nc.dram_tensor("downT", [NE, DFF, D], F32, kind="ExternalInput").ap()
    ident = nc.dram_tensor("ident", [128, 128], F32, kind="ExternalInput").ap()
    rowsel = nc.dram_tensor("rowsel", [NE, NE * 128], F32, kind="ExternalInput").ap()
    x4 = nc.dram_tensor("x4", [D, TPC], F32, kind="ExternalOutput").ap()
    combdbg = nc.dram_tensor("combdbg", [NE, TPC], F32, kind="ExternalOutput").ap()
    accdbg = nc.dram_tensor("accdbg", [D, 1024], F32, kind="ExternalOutput").ap()
    es = ExitStack()
    NDC = DFF // 128
    TH = 1024
    xbf = es.enter_context(nc.sbuf_tensor("xbf", [128, NCH * TPC], BF16))
    rwb = es.enter_context(nc.sbuf_tensor("rwb", [128, NCH * NE], F32))
    rbr = es.enter_context(nc.sbuf_tensor("rbr", [1, NE], F32))
    onesr = es.enter_context(nc.sbuf_tensor("onesr", [1, 128], F32))
    xf = es.enter_context(nc.sbuf_tensor("xf", [128, NCH * 128], F32))
    ones1 = es.enter_context(nc.sbuf_tensor("ones1f", [1, 128], F32))
    idn = es.enter_context(nc.sbuf_tensor("idn", [128, 128], F32))
    rsel = es.enter_context(nc.sbuf_tensor("rsel", [NE, NE * 128], F32))
    probs = es.enter_context(nc.sbuf_tensor("probs", [128, NE], F32))
    m8 = es.enter_context(nc.sbuf_tensor("m8", [128, NE], F32))
    mneg = es.enter_context(nc.sbuf_tensor("mnegM", [128, 1], F32))
    ssum = es.enter_context(nc.sbuf_tensor("ssum", [128, 1], F32))
    comb = es.enter_context(nc.sbuf_tensor("comb", [128, NE], F32))
    comb_fm = es.enter_context(nc.sbuf_tensor("comb_fm", [NE, TPC], F32))
    cbc = es.enter_context(nc.sbuf_tensor("cbc", [128, TH], F32))
    uph = es.enter_context(nc.sbuf_tensor("uph", [128, NCH * 2048], BF16))
    hb = es.enter_context(nc.sbuf_tensor("hb", [128, NDC * TH], BF16))
    dsl = es.enter_context(nc.sbuf_tensor("dsl", [128, NDC * 128], BF16))
    acc = es.enter_context(nc.sbuf_tensor("acc", [128, NCH * TH], F32))
    tmpv = es.enter_context(nc.sbuf_tensor("tmpv", [128, 512], F32))
    xtile = es.enter_context(nc.sbuf_tensor("xtile", [128, 2 * 512], F32))
    psL = es.enter_context(nc.psum_tensor("psL", [128, 512], F32))
    psC = es.enter_context(nc.psum_tensor("psC", [128, 512], F32))
    psU = [es.enter_context(nc.psum_tensor(f"psU{i}", [128, 512], F32)) for i in range(2)]
    psD = [es.enter_context(nc.psum_tensor(f"psD{i}", [128, 512], F32)) for i in range(2)]
    lnb = alloc_ln_bufs(nc, es, reuse=dict(psS=psL[0:1, 0:512], psS2=psC[0:1, 0:512],
                                           psB=psU[0][:], psB2=psU[1][:]))
    p = Prog(nc, es)

    ldx = p.dma("G", xbf[:].rearrange("p (c t) -> p c t", t=TPC),
                xT.rearrange("(c p) t -> p c t", p=128), group="ldx")
    ldrw = p.dma("Y", rwb[:].rearrange("p (c o) -> p c o", o=NE),
                 rw.rearrange("(c p) o -> p c o", p=128), group="ldrw")
    ldrb = p.dma("Y", rbr[:], rbrow, group="ldrb")
    ldi = p.dma("Y", idn[:], ident, group="ldi")
    ldrs = p.dma("Y", rsel[:], rowsel, group="ldrs")
    o1 = p.op("V", lambda v: v.memset(onesr[:], 1.0))
    o2 = p.op("V", lambda v: v.memset(ones1[:], 1.0))
    init = emit_ln_init(p, lnb)

    prev_tr = None
    last_cf = None
    prev_t2 = None
    xf_free = None
    for tc in range(TPC // 128):
        lxf = p.dma("G", xf[:].rearrange("p (c o) -> p c o", o=128),
                    xT.rearrange("(c p) t -> p c t", p=128)[:, :, tc * 128:(tc + 1) * 128],
                    waits=[xf_free], group="xf")
        w0 = [lxf, ldrw, ldrb, o1, prev_t2]
        for ic in range(NCH):
            mmr = p.op("T", (lambda t, ic=ic, tc=tc: t.matmul(
                psL[:, 0:NE],
                xf[:, ic * 128:(ic + 1) * 128],
                rwb[:, ic * NE:(ic + 1) * NE], start=(ic == 0), stop=False)),
                w0 if ic == 0 else [])
            w0 = []
        xf_free = mmr
        mm = p.op("T", (lambda t: t.matmul(
            psL[:, 0:NE], onesr[:], rbr[:], start=False, stop=True)), [])
        t1 = p.op("V", (lambda v: v.reduce_max(mneg[:], psL[:, 0:NE], axis=AX.X, negate=True)),
                  [mm, prev_tr])
        t2 = p.op("S", (lambda s: s.activation(probs[:], psL[:, 0:NE], AF.Exp, bias=mneg[:])), [t1])
        prev_t2 = t2
        t3 = p.op("V", (lambda v: v.reduce_sum(ssum[:], probs[:], axis=AX.X)), [t2])
        t4 = p.op("V", (lambda v: v.reciprocal(ssum[:], ssum[:])), [t3])
        t5 = p.op("V", (lambda v: v.tensor_scalar_mul(probs[:], probs[:], ssum[:])), [t4])
        t6 = p.op("V", (lambda v: v.max(m8[:], probs[:])), [t5])
        t7 = p.op("V", (lambda v: v.scalar_tensor_tensor(
            comb[:], probs[:], m8[:, 1:2], probs[:], ALU.is_ge, ALU.mult)), [t6])
        tr = p.op("T", (lambda t: t.transpose(psC[0:NE, 0:128], comb[:], idn[:])),
                  [t7, ldi, prev_tr])
        cf = p.op("V", (lambda v, tc=tc: v.tensor_copy(
            comb_fm[:, tc * 128:(tc + 1) * 128], psC[0:NE, 0:128])), [tr])
        prev_tr = cf
        last_cf = cf

    upr = upT.rearrange("e (c p) f -> e p c f", p=128)
    dwr = downT.rearrange("e (c p) f -> e p c f", p=128)
    uphv = uph[:].rearrange("p (c f) -> p c f", f=2048)
    dslv = dsl[:].rearrange("p (c o) -> p c o", o=128)
    up_free = None
    dsl_free = [None, None]
    psD_reader = [None, None]
    psU_reader = [None, None]
    prev_cbc = None
    cbc_reader = None
    prev_acc_use = []
    x4_toks = []
    for th in range(2):
        acc_last = {}
        tmp_last = None
        for e in range(NE):
            h_last = []
            for dh in range(2):
                lu = p.dma("G", uphv, upr[e][:, :, dh * 2048:(dh + 1) * 2048], waits=[up_free],
                           group="lu")
                last_ev = None
                for dc in range(16):
                    for ts in range(2):
                        bi = (dc * 2 + ts) % 2
                        pt = psU[bi]
                        w0 = [ldx, lu, psU_reader[bi]] + \
                             (prev_acc_use if (e == 0 and dh == 0 and dc == 0 and ts == 0) else [])
                        mm = None
                        for ic in range(NCH):
                            mm = p.op("T", (lambda t, ic=ic, dc=dc, ts=ts, th=th, pt=pt: t.matmul(
                                pt[:], uphv[:, ic, dc * 128:(dc + 1) * 128],
                                xbf[:, ic * TPC + th * TH + ts * 512: ic * TPC + th * TH + ts * 512 + 512],
                                start=(ic == 0), stop=(ic == NCH - 1))), w0 if ic == 0 else [])
                            w0 = []
                        ev = p.op("S", (lambda s, dh=dh, dc=dc, ts=ts, pt=pt: s.activation(
                            hb[:, (dh * 16 + dc) * TH + ts * 512: (dh * 16 + dc) * TH + ts * 512 + 512],
                            pt[:], AF.Gelu)), [mm])
                        psU_reader[bi] = ev
                        last_ev = ev
                up_free = last_ev
                h_last.append(last_ev)
            cb_toks = []
            for ts in range(2):
                bcm = p.op("T", (lambda t, e=e, th=th, ts=ts: t.matmul(
                    psC[:, 0:512], rsel[:, e * 128:(e + 1) * 128],
                    comb_fm[0:NE, th * TH + ts * 512: th * TH + ts * 512 + 512],
                    start=True, stop=True)), [last_cf, ldrs, prev_cbc] + cb_toks +
                    ([cbc_reader] if cbc_reader else []))
                cbc_t = p.op("V", (lambda v, ts=ts: v.tensor_copy(
                    cbc[:, ts * 512:(ts + 1) * 512], psC[:, 0:512])), [bcm])
                prev_cbc = cbc_t
                cb_toks.append(cbc_t)
            last_mm = None
            for oc in range(NCH):
                par = 0
                ld2 = p.dma("G", dslv, dwr[e][:, :, oc * 128:(oc + 1) * 128],
                            waits=[dsl_free[par]], group="dsl")
                for ts in range(2):
                    pt = psD[ts]
                    w0 = (h_last if oc == 0 else []) + [ld2, psD_reader[ts]] + \
                         (prev_acc_use if (oc == 0 and e == 0) else [])
                    mm = None
                    for kc in range(NDC):
                        mm = p.op("T", (lambda t, kc=kc, ts=ts, pt=pt: t.matmul(
                            pt[:], dslv[:, kc],
                            hb[:, kc * TH + ts * 512: kc * TH + ts * 512 + 512],
                            start=(kc == 0), stop=(kc == NDC - 1))), w0 if kc == 0 else [])
                        w0 = []
                    last_mm = mm
                    aslice = acc[:, oc * TH + ts * 512: oc * TH + ts * 512 + 512]
                    if e == 0:
                        rd = p.op("V", (lambda v, aslice=aslice, ts=ts, pt=pt: v.tensor_mul(
                            aslice, pt[:], cbc[:, ts * 512:(ts + 1) * 512])), [mm] + cb_toks)
                        acc_last[(oc, ts)] = rd
                    else:
                        rd = p.op("V", (lambda v, ts=ts, pt=pt: v.tensor_mul(
                            tmpv[:], pt[:], cbc[:, ts * 512:(ts + 1) * 512])),
                            [mm, tmp_last] + cb_toks)
                        at = p.op("G", (lambda g, aslice=aslice: g.tensor_add(
                            aslice, aslice, tmpv[:])), [rd, acc_last[(oc, ts)]])
                        tmp_last = at
                        acc_last[(oc, ts)] = at
                    psD_reader[ts] = rd
                    cbc_reader = rd
                dsl_free[par] = last_mm
        if th == 0:
            dbg_tok = p.dma("G", accdbg.rearrange("(c p) t -> p c t", p=128),
                            acc[:].rearrange("p (c t) -> p c t", t=TH),
                            waits=[acc_last[(c_, t_)] for c_ in range(NCH) for t_ in range(2)],
                            group="accdbg")
        def a_ap(c, tt, th=th):
            return acc[:, c * TH + tt * 512: c * TH + tt * 512 + 512]
        res_toks = []
        xl_prev = [None, None]
        for c in range(NCH):
            for tt in range(2):
                par = tt % 2
                lx = p.dma("G", xtile[:, par * 512:(par + 1) * 512],
                           xT.rearrange("(c p) t -> p c t", p=128)[:, c,
                                        th * TH + tt * 512: th * TH + tt * 512 + 512],
                           waits=[xl_prev[par]], group=f"xt{par}")
                ad = p.op("V", (lambda v, c=c, tt=tt, par=par: v.tensor_add(
                    a_ap(c, tt), a_ap(c, tt), xtile[:, par * 512:(par + 1) * 512])),
                    [lx, acc_last[(c, tt)], (p.gtotal("accdbg") if th == 0 else None)])
                xl_prev[par] = ad
                res_toks.append(ad)
        toks, last_ln = emit_ln(p, lnb, a_ap, TH, out_fn=a_ap, pre_waits=[res_toks[-1]],
                                init_toks=init)
        for c in range(NCH):
            for tt in range(2):
                p.dma("G", x4.rearrange("(c p) t -> p c t", p=128)[:, c,
                                        th * TH + tt * 512: th * TH + tt * 512 + 512],
                      a_ap(c, tt), waits=[toks[(c, tt)]], group="x4out")
        prev_acc_use = [last_ln, p.gtotal("x4out")]
    p.dma("G", combdbg, comb_fm[0:NE, :], waits=[last_cf], group="cdbg")
    p.emit()
    es.close()
    return nc



def build_addln():
    nc = bass.Bass("TRN2", target_bir_lowering=False)
    xT = nc.dram_tensor("xT", [D, TPC], F32, kind="ExternalInput").ap()
    yT = nc.dram_tensor("yT", [D, TPC], F32, kind="ExternalInput").ap()
    xn = nc.dram_tensor("xn", [D, TPC], F32, kind="ExternalOutput").ap()
    es = ExitStack()
    xr = es.enter_context(nc.sbuf_tensor("xr", [128, NCH * TPC], F32))
    yb = es.enter_context(nc.sbuf_tensor("yb", [128, NCH * TPC], F32))
    lnb = alloc_ln_bufs(nc, es)
    p = Prog(nc, es)
    ldx = p.dma("G", xr[:].rearrange("p (c t) -> p c t", t=TPC), xT.rearrange("(c p) t -> p c t", p=128), group="ldx")
    ldy = p.dma("G", yb[:].rearrange("p (c t) -> p c t", t=TPC), yT.rearrange("(c p) t -> p c t", p=128), group="ldy")
    init = emit_ln_init(p, lnb)
    add_toks = []
    for c in range(NCH):
        for tt in range(NTT):
            ad = p.op("V", (lambda v, c=c, tt=tt: v.tensor_add(
                xr[:, c * TPC + tt * 512: c * TPC + (tt + 1) * 512],
                xr[:, c * TPC + tt * 512: c * TPC + (tt + 1) * 512],
                yb[:, c * TPC + tt * 512: c * TPC + (tt + 1) * 512])),
                [ldx, ldy] if (c == 0 and tt == 0) else [])
            add_toks.append(ad)

    def x_ap(c, tt):
        return xr[:, c * TPC + tt * 512: c * TPC + (tt + 1) * 512]

    toks, last = emit_ln(p, lnb, x_ap, TPC, out_fn=x_ap, pre_waits=[add_toks[-1]], init_toks=init)
    for c in range(NCH):
        for tt in range(NTT):
            p.dma("G", xn.rearrange("(c p) t -> p c t", p=128)[:, c, tt * 512:(tt + 1) * 512],
                  x_ap(c, tt), waits=[toks[(c, tt)]], group="xout")
    p.emit()
    es.close()
    return nc



def _cache_build(key, fn):
    if key not in _NC_CACHE:
        _NC_CACHE[key] = fn()
    return _NC_CACHE[key]


def _fm(x):
    return np.ascontiguousarray(x.T)


def host_tri_masks():
    ks = np.arange(128)
    qs = np.arange(512)
    m = np.zeros((4, 128, 512), dtype=BF)
    for s in range(4):
        m[s] = ((128 * s + ks[:, None]) <= qs[None, :]).astype(BF)
    return m


def host_local_masks(first_window_of_seq):
    ks = np.arange(128)
    qs = np.arange(512)
    out = np.zeros((4, 8, 128, 512), dtype=BF)
    for w in range(4):
        for kc in range(2):
            for s in range(4):
                klocal = kc * 512 + s * 128 + ks[:, None]
                ql = qs[None, :]
                ok = (klocal > ql) & (klocal <= ql + 512)
                if w == 0 and first_window_of_seq:
                    ok = ok & (klocal >= 512)
                out[w, kc * 4 + s] = ok.astype(BF)
    return out.reshape(32, 128, 512)


def host_rotary_tables():
    DH, WIN = 64, 512
    half = np.arange(0, DH, 2, dtype=np.float64)
    inv_freq = 1.0 / (10000.0 ** (half / DH))
    t = np.arange(T, dtype=np.float64)
    freqs = t[:, None] * inv_freq[None, :]
    freqs = np.concatenate([freqs, freqs], axis=1)
    base = (half + 0.4 * DH) / (1.4 * DH)
    power = (t - T // 2) / (WIN // 2)
    scale = base[None, :] ** power[:, None]
    scale = np.concatenate([scale, scale], axis=1)
    cos, sin = np.cos(freqs), np.sin(freqs)
    AQ = (cos * scale).T.astype(np.float32)
    BQr = (sin * scale).T.astype(np.float32)
    AK = (cos / scale).T.astype(np.float32)
    BKr = (sin / scale).T.astype(np.float32)
    sgn = np.concatenate([-np.ones(32), np.ones(32)]).astype(np.float32)[:, None]
    return AQ, BQr * sgn, AK, BKr * sgn


def host_E2():
    e = np.zeros((128, 128), dtype=np.float32)
    e[0:64, 0:64] = 1.0
    e[64:128, 64:128] = 1.0
    return e


_IDENT = np.eye(128, dtype=np.float32)
_ROWSEL = np.zeros((8, 8 * 128), dtype=np.float32)
for _e in range(8):
    _ROWSEL[_e, _e * 128:(_e + 1) * 128] = 1.0


def run_dense_attn_block(x, Wq, bq, Wk, bk, Wv, bv, Wo, bo):
    nc1 = _cache_build("qkv", build_qkv_launch)
    in1 = []
    for c in range(NCORES):
        b, s = divmod(c, 4)
        sl = x[b, s * TPC:(s + 1) * TPC]
        in1.append(dict(xT=_fm(sl), wq=Wq, wk=Wk, wv=Wv,
                        bq=np.ascontiguousarray(bq.reshape(NCH, 128).T),
                        bk=np.ascontiguousarray(bk.reshape(NCH, 128).T),
                        bvrow=bv.reshape(1, D)))
    r1 = run_bass_kernel_spmd(nc1, in1, core_ids=list(range(NCORES)))
    QF = np.zeros((B, D, T), dtype=BF)
    KF = np.zeros((B, D, T), dtype=BF)
    VT = np.zeros((B, T, D), dtype=BF)
    for c in range(NCORES):
        b, s = divmod(c, 4)
        QF[b][:, s * TPC:(s + 1) * TPC] = r1.results[c]["qx"]
        KF[b][:, s * TPC:(s + 1) * TPC] = r1.results[c]["kx"]
        VT[b][s * TPC:(s + 1) * TPC] = r1.results[c]["vx"]
    nc2 = _cache_build("flash", build_flash_dense)
    tri = host_tri_masks()
    in2 = []
    for c in range(NCORES):
        pairs = [(c * 4 + i) for i in range(4)]
        q4 = np.stack([np.ascontiguousarray(QF[pr // 16][(pr % 16) * 64:(pr % 16) * 64 + 64]) for pr in pairs])
        k4 = np.stack([np.ascontiguousarray(KF[pr // 16][(pr % 16) * 64:(pr % 16) * 64 + 64]) for pr in pairs])
        v4 = np.stack([np.ascontiguousarray(VT[pr // 16][:, (pr % 16) * 64:(pr % 16) * 64 + 64]) for pr in pairs])
        in2.append(dict(q4=q4, k4=k4, v4=v4, ident=_IDENT, trimask=tri))
    r2 = run_bass_kernel_spmd(nc2, in2, core_ids=list(range(NCORES)))
    OF = np.zeros((B, D, T), dtype=BF)
    for c in range(NCORES):
        for i in range(4):
            pr = c * 4 + i
            OF[pr // 16][(pr % 16) * 64:(pr % 16) * 64 + 64] = r2.results[c]["o4"][i]
    nc3 = _cache_build("projln", build_proj_ln)
    in3 = []
    for c in range(NCORES):
        b, s = divmod(c, 4)
        in3.append(dict(oT=np.ascontiguousarray(OF[b][:, s * TPC:(s + 1) * TPC]),
                        wo=Wo, bo=np.ascontiguousarray(bo.reshape(NCH, 128).T),
                        xT=_fm(x[b, s * TPC:(s + 1) * TPC])))
    r3 = run_bass_kernel_spmd(nc3, in3, core_ids=list(range(NCORES)))
    xo = np.zeros((B, T, D), dtype=np.float32)
    for c in range(NCORES):
        b, s = divmod(c, 4)
        xo[b, s * TPC:(s + 1) * TPC] = r3.results[c]["xn"].T
    return xo


def run_lffn_layer(x, wqkv_l, qs_l, ks_l, wo_l, lg_l, lb_l, w1_l, b1_l, w2_l, b2_l):
    AQ, BQ, AK, BK = host_rotary_tables()
    E2 = host_E2()
    HID = 2730
    w1p = np.zeros((D, 2 * HID_P), dtype=np.float32)
    w1p[:, 0:HID] = w1_l[:, 0:HID]
    w1p[:, HID_P:HID_P + HID] = w1_l[:, HID:2 * HID]
    b1p = np.zeros(2 * HID_P, dtype=np.float32)
    b1p[0:HID] = b1_l[0:HID]
    b1p[HID_P:HID_P + HID] = b1_l[HID:2 * HID]
    w2p = np.zeros((HID_P, D), dtype=np.float32)
    w2p[0:HID] = w2_l

    nc = _cache_build("lffn_qkv", build_lffn_qkv)
    ins = []
    for c in range(NCORES):
        b, s = divmod(c, 4)
        sl = x[b, s * TPC:(s + 1) * TPC]
        xprev = (x[b, s * TPC - 1] if s > 0 else np.zeros(D, dtype=np.float32)).reshape(D, 1)
        t0 = s * TPC
        tiles = {n: np.ascontiguousarray(np.tile(tab[:, t0:t0 + TPC], (2, 1)))
                 for n, tab in (("AQ", AQ), ("BQ", BQ), ("AK", AK), ("BK", BK))}
        ins.append(dict(xT=_fm(sl), xprevT=np.ascontiguousarray(xprev), wqkv=wqkv_l,
                        qs2=np.tile(qs_l, 2).reshape(128, 1).astype(np.float32),
                        ks2=np.tile(ks_l, 2).reshape(128, 1).astype(np.float32),
                        E2=E2, **tiles))
    r1 = run_bass_kernel_spmd(nc, ins, core_ids=list(range(NCORES)))
    QF = np.zeros((B, INNER, T), dtype=BF)
    KF = np.zeros((B, INNER, T), dtype=BF)
    VT = np.zeros((B, T, INNER), dtype=BF)
    for c in range(NCORES):
        b, s = divmod(c, 4)
        QF[b][:, s * TPC:(s + 1) * TPC] = r1.results[c]["qx"]
        KF[b][:, s * TPC:(s + 1) * TPC] = r1.results[c]["kx"]
        VT[b][s * TPC:(s + 1) * TPC] = r1.results[c]["vx"]

    nc2 = _cache_build("lffn_attn", build_lffn_attn)
    ins2 = []
    for c in range(NCORES):
        b, s = divmod(c, 4)
        t0 = s * TPC
        if s > 0:
            kfb = KF[b][:, t0 - 512: t0 + TPC]
            vfb = VT[b][t0 - 512: t0 + TPC]
        else:
            kfb = np.concatenate([np.zeros((INNER, 512), dtype=BF), KF[b][:, 0:TPC]], axis=1)
            vfb = np.concatenate([np.zeros((512, INNER), dtype=BF), VT[b][0:TPC]], axis=0)
        ins2.append(dict(qx=np.ascontiguousarray(QF[b][:, t0:t0 + TPC]),
                         kfb=np.ascontiguousarray(kfb), vfb=np.ascontiguousarray(vfb),
                         masks=host_local_masks(first_window_of_seq=(s == 0)),
                         ident=_IDENT))
    r2 = run_bass_kernel_spmd(nc2, ins2, core_ids=list(range(NCORES)))

    nc3 = _cache_build("lffn_proj", build_lffn_proj)
    ins3 = []
    for c in range(NCORES):
        b, s = divmod(c, 4)
        ins3.append(dict(oT=r2.results[c]["ox"], wo=wo_l,
                         xT=_fm(x[b, s * TPC:(s + 1) * TPC])))
    r3 = run_bass_kernel_spmd(nc3, ins3, core_ids=list(range(NCORES)))
    x2 = np.zeros((B, T, D), dtype=np.float32)
    for c in range(NCORES):
        b, s = divmod(c, 4)
        x2[b, s * TPC:(s + 1) * TPC] = r3.results[c]["x2"].T

    nc4 = _cache_build("lffn_ffn", build_lffn_ffn)
    ins4 = []
    for c in range(NCORES):
        b, s = divmod(c, 4)
        xprev = (x2[b, s * TPC - 1] if s > 0 else np.zeros(D, dtype=np.float32)).reshape(D, 1)
        ins4.append(dict(x2T=_fm(x2[b, s * TPC:(s + 1) * TPC]),
                         x2prevT=np.ascontiguousarray(xprev),
                         lg=np.ascontiguousarray(lg_l.reshape(NCH, 128).T),
                         lb=np.ascontiguousarray(lb_l.reshape(NCH, 128).T),
                         w1p=w1p,
                         b1p=np.ascontiguousarray(b1p.reshape(2 * HID_P // 128, 128).T),
                         w2p=w2p,
                         b2=np.ascontiguousarray(b2_l.reshape(NCH, 128).T)))
    r4 = run_bass_kernel_spmd(nc4, ins4, core_ids=list(range(NCORES)))
    x3 = np.zeros((B, T, D), dtype=np.float32)
    for c in range(NCORES):
        b, s = divmod(c, 4)
        x3[b, s * TPC:(s + 1) * TPC] = r4.results[c]["x3"].T
    return x3


def run_moe_block(x, router_w, router_b, expert_up, expert_down):
    upTh = np.ascontiguousarray(expert_up.transpose(0, 2, 1))
    downTh = np.ascontiguousarray(expert_down.transpose(0, 2, 1))
    nc = _cache_build("moe", build_moe)
    ins = []
    for c in range(NCORES):
        b, s = divmod(c, 4)
        ins.append(dict(xT=_fm(x[b, s * TPC:(s + 1) * TPC]), rw=router_w,
                        rbrow=router_b.reshape(1, NE), upT=upTh, downT=downTh,
                        ident=_IDENT, rowsel=_ROWSEL))
    r = run_bass_kernel_spmd(nc, ins, core_ids=list(range(NCORES)))
    xo = np.zeros((B, T, D), dtype=np.float32)
    for c in range(NCORES):
        b, s = divmod(c, 4)
        xo[b, s * TPC:(s + 1) * TPC] = r.results[c]["x4"].T
    return xo


def run_addln(x, y):
    nc = _cache_build("addln", build_addln)
    ins = []
    for c in range(NCORES):
        b, s = divmod(c, 4)
        ins.append(dict(xT=_fm(x[b, s * TPC:(s + 1) * TPC]), yT=_fm(y[b, s * TPC:(s + 1) * TPC])))
    r = run_bass_kernel_spmd(nc, ins, core_ids=list(range(NCORES)))
    xo = np.zeros((B, T, D), dtype=np.float32)
    for c in range(NCORES):
        b, s = divmod(c, 4)
        xo[b, s * TPC:(s + 1) * TPC] = r.results[c]["xn"].T
    return xo


def kernel(L0_attn_x, L0_attn_Wq, L0_attn_bq, L0_attn_Wk, L0_attn_bk, L0_attn_Wv,
           L0_attn_bv, L0_attn_Wo, L0_attn_bo, L0_ffn_wqkv, L0_ffn_q_scale,
           L0_ffn_k_scale, L0_ffn_wo, L0_ffn_ln_g, L0_ffn_ln_b, L0_ffn_w1, L0_ffn_b1,
           L0_ffn_w2, L0_ffn_b2, L1_attn_w_attn, L1_attn_b_attn, L1_attn_w_proj,
           L1_attn_b_proj, L1_ffn_router_w, L1_ffn_router_b, L1_ffn_expert_up,
           L1_ffn_expert_down):
    g = {k: np.ascontiguousarray(np.asarray(v), dtype=None) for k, v in list(locals().items())
         if k != "g"}
    x = np.asarray(g["L0_attn_x"], dtype=np.float32)
    x = run_dense_attn_block(x, g["L0_attn_Wq"], g["L0_attn_bq"], g["L0_attn_Wk"],
                             g["L0_attn_bk"], g["L0_attn_Wv"], g["L0_attn_bv"],
                             g["L0_attn_Wo"], g["L0_attn_bo"])
    y = x
    for l in range(2):
        y = run_lffn_layer(y, g["L0_ffn_wqkv"][l], g["L0_ffn_q_scale"][l],
                           g["L0_ffn_k_scale"][l], g["L0_ffn_wo"][l], g["L0_ffn_ln_g"][l],
                           g["L0_ffn_ln_b"][l], g["L0_ffn_w1"][l], g["L0_ffn_b1"][l],
                           g["L0_ffn_w2"][l], g["L0_ffn_b2"][l])
    x = run_addln(x, y)
    wA = np.asarray(g["L1_attn_w_attn"], dtype=np.float32)
    bA = np.asarray(g["L1_attn_b_attn"], dtype=np.float32)
    x = run_dense_attn_block(x, np.ascontiguousarray(wA[:, 0:D]), bA[0:D],
                             np.ascontiguousarray(wA[:, D:2 * D]), bA[D:2 * D],
                             np.ascontiguousarray(wA[:, 2 * D:3 * D]), bA[2 * D:3 * D],
                             g["L1_attn_w_proj"], g["L1_attn_b_proj"])
    x = run_moe_block(x, g["L1_ffn_router_w"], g["L1_ffn_router_b"],
                      np.asarray(g["L1_ffn_expert_up"], dtype=np.float32),
                      np.asarray(g["L1_ffn_expert_down"], dtype=np.float32))
    return x
